# version 07
# baseline (speedup 1.0000x reference)
import numpy as np
import ml_dtypes
import concourse.bass as bass
import concourse.mybir as mybir
from concourse.ap import AP
from concourse.bass_utils import run_bass_kernel_spmd

F32 = mybir.dt.float32
BF16 = mybir.dt.bfloat16
AF = mybir.ActivationFunctionType
ALU = mybir.AluOpType

D = 1024
NB = 2
TQ = 512
EPS = 1e-5


def build_nc(debug=False):
    nc = bass.Bass("TRN2", target_bir_lowering=False)

    x_d = nc.dram_tensor("x", [NB * 2048, D], F32, kind="ExternalInput").ap()
    wq_d = nc.dram_tensor("wq", [D, D], F32, kind="ExternalInput").ap()
    bq_d = nc.dram_tensor("bq", [D], F32, kind="ExternalInput").ap()
    wk_d = nc.dram_tensor("wk", [D, 128], F32, kind="ExternalInput").ap()
    bk_d = nc.dram_tensor("bk", [128], F32, kind="ExternalInput").ap()
    wv_d = nc.dram_tensor("wv", [D, 128], F32, kind="ExternalInput").ap()
    bv_d = nc.dram_tensor("bv", [128], F32, kind="ExternalInput").ap()
    wo_d = nc.dram_tensor("wo", [D, D], F32, kind="ExternalInput").ap()
    bo_d = nc.dram_tensor("bo", [D], F32, kind="ExternalInput").ap()
    g1r_d = nc.dram_tensor("g1r", [32, 128], F32, kind="ExternalInput").ap()
    g2r_d = nc.dram_tensor("g2r", [4, 16, 16], F32, kind="ExternalInput").ap()
    g1i_d = nc.dram_tensor("g1i", [32, 128], F32, kind="ExternalInput").ap()
    g2i_d = nc.dram_tensor("g2i", [4, 16, 16], F32, kind="ExternalInput").ap()
    br_d = nc.dram_tensor("br", [512], F32, kind="ExternalInput").ap()
    bi_d = nc.dram_tensor("bi", [512], F32, kind="ExternalInput").ap()
    ident_d = nc.dram_tensor("ident", [128, 128], BF16, kind="ExternalInput").ap()
    maskl_d = nc.dram_tensor("maskl", [128, 512], BF16, kind="ExternalInput").ap()
    maskr_d = nc.dram_tensor("maskr", [128, 512], BF16, kind="ExternalInput").ap()
    r32_d = nc.dram_tensor("r32", [32, 512], BF16, kind="ExternalInput").ap()
    d32_d = nc.dram_tensor("d32", [32, 512], BF16, kind="ExternalInput").ap()
    r16_d = nc.dram_tensor("r16", [16, 512], BF16, kind="ExternalInput").ap()
    d16_d = nc.dram_tensor("d16", [16, 128], BF16, kind="ExternalInput").ap()
    y_d = nc.dram_tensor("y", [NB * 2048, D], F32, kind="ExternalOutput").ap()
    dbg = {}
    if debug:
        dbg_spec = [("dqt", [128, 8192]), ("dkt", [128, 2304]),
                    ("dvb", [128, 2340]), ("dex", [128, 6144]),
                    ("dat", [128, 1024]), ("dp1", [128, 1024]),
                    ("dx1", [128, 1024]), ("dwr", [128, 2048]),
                    ("dwi", [128, 2048]), ("dst", [128, 16]), ("drs", [128, 2])]
        for nm, sh in dbg_spec:
            dbg[nm] = nc.dram_tensor(nm, sh, F32, kind="ExternalOutput").ap()

    def sb(name, shape, dt):
        return nc.alloc_sbuf_tensor(name, shape, dt).ap()

    xs = [sb(f"xs{i}", [128, D], F32) for i in range(2)]
    x_tm = sb("x_tm", [128, 8 * D], BF16)
    XT = sb("XT", [128, 4096], BF16)
    QT = sb("QT", [128, 8192], BF16)
    KT = sb("KT", [128, 2304], BF16)
    VB = sb("VB", [128, 18 * 130], BF16)
    EX = sb("EX", [128, 24 * 512], BF16)
    vstage = sb("vstage", [128, 512], BF16)
    attn = sb("attn", [128, 2 * 1024], BF16)
    attnT = sb("attnT", [128, 2 * 1024], BF16)
    x1_tm = sb("x1_tm", [128, 2 * 1024], BF16)
    x1T = sb("x1T", [128, 2 * 1024], BF16)
    pre1 = sb("pre1", [128, 2 * 1024], F32)
    pre2 = sb("pre2", [128, 2 * 1024], F32)
    outst = sb("outst", [128, 2 * 1024], F32)
    scr = sb("scr", [128, 1024], BF16)
    st1 = sb("st1", [128, 2 * 8], F32)
    st2 = sb("st2", [128, 2 * 8], F32)
    rstd1 = sb("rstd1", [128, 2], F32)
    rstd2 = sb("rstd2", [128, 2], F32)
    Rrec = sb("Rrec", [128, 2 * 16], F32)
    wq_sb = sb("wq_sb", [128, 8192], BF16)
    wk_sb = sb("wk_sb", [128, 1024], BF16)
    wv_sb = sb("wv_sb", [128, 1024], BF16)
    wo_sb = sb("wo_sb", [128, 8192], BF16)
    wrt = sb("wrt", [128, 2048], BF16)
    wit = sb("wit", [128, 2048], BF16)
    witn = sb("witn", [128, 2048], BF16)
    wacc = sb("wacc", [128, 2048], F32)
    wtmp = sb("wtmp", [128, 512], F32)
    bq_sb = sb("bq_sb", [128, 8], F32)
    bk_sb = sb("bk_sb", [128, 1], F32)
    bv_sb = sb("bv_sb", [128, 1], F32)
    bo_row = sb("bo_row", [1, 1024], BF16)
    beta_row = sb("beta_row", [1, 1024], BF16)
    brow_st = sb("brow_st", [1, 1024], F32)
    ones_row = sb("ones_row", [1, 128], BF16)
    expbias = sb("expbias", [128, 1], F32)
    neghalf = sb("neghalf", [128, 1], F32)
    identb = sb("identb", [128, 128], BF16)
    maskl = sb("maskl_s", [128, 512], BF16)
    maskr = sb("maskr_s", [128, 512], BF16)
    r32 = sb("r32_s", [32, 512], BF16)
    d32 = sb("d32_s", [32, 512], BF16)
    r16 = sb("r16_s", [16, 512], BF16)
    d16 = sb("d16_s", [16, 128], BF16)
    g1b = sb("g1b", [32, 128], BF16)
    g2b = sb("g2b", [16, 64], BF16)
    g1b2 = sb("g1b2", [32, 128], BF16)
    g2b2 = sb("g2b2", [16, 64], BF16)
    gst = sb("gst", [32, 128], F32)
    gst2 = sb("gst2", [16, 64], F32)
    t1s = sb("t1s", [32, 512], BF16)
    t2s = sb("t2s", [16, 512], BF16)
    f2s = sb("f2s", [128, 512], BF16)
    wstage = [sb("wstage0", [128, D], F32)]

    pS = [nc.alloc_psum_tensor(f"pS{i}", [128, 512], F32).ap() for i in range(4)]
    pW = [nc.alloc_psum_tensor(f"pW{i}", [128, 512], F32).ap() for i in range(2)]
    pB = nc.alloc_psum_tensor("pB", [128, 512], F32).ap()
    pT = nc.alloc_psum_tensor("pT", [128, 1024], BF16).ap()

    SEMS = ["t", "s", "v", "g", "dx", "dw", "do"]
    SEMH = {k: nc.alloc_semaphore("sem_" + k) for k in SEMS}

    class Em:
        def __init__(self, cur, handle):
            self.cur = cur
            self.h = handle
            self.c = {k: 0 for k in SEMS}
            self.ev = {}

        def on(self, name):
            return self.cur == name

        def inc(self, sem, instr, n=1):
            self.c[sem] += n
            if instr is not None:
                instr.then_inc(SEMH[sem], n)
            return (sem, self.c[sem])

        def wait(self, ev):
            if ev is None:
                return
            if isinstance(ev, list):
                for e in ev:
                    self.wait(e)
                return
            self.h.wait_ge(SEMH[ev[0]], ev[1])

        def res(self, name):
            return self.ev.setdefault(name, {"r": None, "f": None})

    def ap3(base_ap, offset, dims):
        return AP(base_ap.tensor, offset, dims)

    def schedule(E):
        cur = E.cur
        ten = E.h if cur == "t" else None
        sca = E.h if cur == "s" else None
        vec = E.h if cur == "v" else None
        gps = E.h if cur == "g" else None
        syn = E.h if cur == "sync" else None

        if cur == "v":
            vec.memset(expbias[:], -32.0)
            vec.memset(neghalf[:], -0.5)
            vec.memset(ones_row[:], 1.0)
            vec.memset(VB[:], 0.0)
            vec.memset(ap3(VB, 64, [[2340, 128], [130, 18]]), 1.0)
            vec.memset(ap3(VB, 129, [[2340, 128], [130, 18]]), 1.0)
            vec.memset(KT[:, 0:128], 0.0)
            i = vec.memset(KT[:, 2176:2304], 0.0)
        ev_init_v = E.inc("v", i if cur == "v" else None)

        for sbuf, dram in [(identb, ident_d), (maskl, maskl_d), (maskr, maskr_d),
                           (r32, r32_d), (d32, d32_d), (r16, r16_d), (d16, d16_d)]:
            i = syn.dma_start(sbuf, dram) if cur == "sync" else None
            E.inc("dw", i, 16)
        i = syn.dma_start(gst[:], g1r_d) if cur == "sync" else None
        E.inc("dw", i, 16)
        g2r_src = AP(g2r_d.tensor, 0, [[16, 16], [256, 4], [1, 16]])
        i = syn.dma_start(gst2[:], g2r_src,
                          allow_slow_non_contiguous=True) if cur == "sync" else None
        ev_g_r = E.inc("dw", i, 16)
        i = syn.dma_start(bq_sb[:], bq_d.rearrange("(c p) -> p c", p=128),
                          allow_slow_non_contiguous=True) if cur == "sync" else None
        E.inc("dw", i, 16)
        i = syn.dma_start(bk_sb[:], bk_d.rearrange("(p o) -> p o", o=1),
                          allow_slow_non_contiguous=True) if cur == "sync" else None
        E.inc("dw", i, 16)
        i = syn.dma_start(bv_sb[:], bv_d.rearrange("(p o) -> p o", o=1),
                          allow_slow_non_contiguous=True) if cur == "sync" else None
        E.inc("dw", i, 16)
        i = syn.dma_start(brow_st[0:1, :], bo_d.rearrange("(o f) -> o f", o=1)) \
            if cur == "sync" else None
        ev_bo = E.inc("dw", i, 16)
        if cur == "s":
            sca.wait_ge(SEMH["dw"], ev_bo[1])
            i = sca.copy(bo_row[:], brow_st[:])
        ev_borow = E.inc("s", i if cur == "s" else None)
        if cur == "sync":
            syn.wait_ge(SEMH["s"], ev_borow[1])
        i = syn.dma_start(brow_st[0:1, 0:512], br_d.rearrange("(o f) -> o f", o=1)) \
            if cur == "sync" else None
        E.inc("dw", i, 16)
        i = syn.dma_start(brow_st[0:1, 512:1024], bi_d.rearrange("(o f) -> o f", o=1)) \
            if cur == "sync" else None
        ev_beta_ld = E.inc("dw", i, 16)
        if cur == "s":
            sca.wait_ge(SEMH["dw"], ev_beta_ld[1])
            i = sca.copy(beta_row[:], brow_st[:])
        ev_betarow = E.inc("s", i if cur == "s" else None)

        stage_free = [None, None]
        stages = [wstage[0], xs[0]]
        ev_wconv = None
        wconv_ev = {}
        widx = 0
        for nm, dram, dest, nch in [("wq", wq_d, wq_sb, 8), ("wo", wo_d, wo_sb, 8),
                                    ("wk", wk_d, wk_sb, 1), ("wv", wv_d, wv_sb, 1)]:
            for c in range(nch):
                sidx = widx % 2
                widx += 1
                if cur == "sync":
                    E.wait(stage_free[sidx])
                if nm in ("wk", "wv"):
                    src = AP(dram.tensor, 0, [[128, 128], [128 * 128, 8], [1, 128]])
                else:
                    src = dram[c * 128:(c + 1) * 128, :]
                i = syn.dma_start(stages[sidx][:], src) if cur == "sync" else None
                ev_ld = E.inc("dw", i, 16)
                if cur == "s":
                    sca.wait_ge(SEMH["dw"], ev_ld[1])
                    i = sca.copy(dest[:, c * 1024:(c + 1) * 1024], stages[sidx][:])
                ev_cv = E.inc("s", i if cur == "s" else None)
                stage_free[sidx] = ev_cv
                ev_wconv = ev_cv
                wconv_ev[nm] = ev_cv
                if sidx == 1:
                    E.res("xs0")["f"] = ev_cv

        if cur == "s":
            sca.wait_ge(SEMH["dw"], ev_g_r[1])
            sca.copy(g1b[:], gst[:])
            i = sca.copy(g2b[:], gst2[:])
        ev_gconv1 = E.inc("s", i if cur == "s" else None)
        if cur == "sync":
            syn.wait_ge(SEMH["s"], ev_gconv1[1])
        i = syn.dma_start(gst[:], g1i_d) if cur == "sync" else None
        E.inc("dw", i, 16)
        g2i_src = AP(g2i_d.tensor, 0, [[16, 16], [256, 4], [1, 16]])
        i = syn.dma_start(gst2[:], g2i_src,
                          allow_slow_non_contiguous=True) if cur == "sync" else None
        ev_g_i = E.inc("dw", i, 16)
        if cur == "s":
            sca.wait_ge(SEMH["dw"], ev_g_i[1])
            sca.copy(g1b2[:], gst[:])
            i = sca.copy(g2b2[:], gst2[:])
        ev_gconv2 = E.inc("s", i if cur == "s" else None)

        ev_dwall = ("dw", E.c["dw"])
        ev_wrt = ev_wit = None
        for xi, (g1x, g2x, ev_gc) in enumerate([(g1b, g2b, ev_gconv1),
                                                (g1b2, g2b2, ev_gconv2)]):
            for c in range(4):
                if cur == "t":
                    if xi == 0 and c == 0:
                        ten.wait_ge(SEMH["dw"], ev_g_r[1])
                    E.wait(E.res("pW0b")["f"])
                    E.wait(ev_gc)
                i = ten.matmul(pW[0][0:32, :], g1x[:, c * 32:(c + 1) * 32], r32[:],
                               start=True, stop=True) if cur == "t" else None
                ev = E.inc("t", i)
                if cur == "s":
                    sca.wait_ge(SEMH["t"], ev[1])
                    i = sca.copy(t1s[:], pW[0][0:32, :])
                ev_t1 = E.inc("s", i if cur == "s" else None)
                if cur == "t":
                    E.wait(E.res("pW1b")["f"])
                i = ten.matmul(pW[1][0:16, :], g2x[:, c * 16:(c + 1) * 16], r16[:],
                               start=True, stop=True) if cur == "t" else None
                ev = E.inc("t", i)
                if cur == "s":
                    sca.wait_ge(SEMH["t"], ev[1])
                    i = sca.copy(t2s[:], pW[1][0:16, :])
                ev_t2 = E.inc("s", i if cur == "s" else None)
                if cur == "t":
                    E.wait(ev_t2)
                i = ten.matmul(pW[1][:, :], d16[:], t2s[:], start=True, stop=True) \
                    if cur == "t" else None
                ev = E.inc("t", i)
                if cur == "s":
                    sca.wait_ge(SEMH["t"], ev[1])
                    i = sca.copy(f2s[:], pW[1][:, :])
                ev_f2 = E.inc("s", i if cur == "s" else None)
                E.res("pW1b")["f"] = ev_f2
                for tt in range(4):
                    if cur == "t":
                        E.wait(ev_t1)
                        E.wait(E.res("pW0b")["f"])
                    i = ten.matmul(pW[0][:, :], d32[:, tt * 128:(tt + 1) * 128], t1s[:],
                                   start=True, stop=True) if cur == "t" else None
                    ev_f1 = E.inc("t", i)
                    if cur == "v":
                        vec.wait_ge(SEMH["t"], ev_f1[1])
                        vec.wait_ge(SEMH["s"], ev_f2[1])
                        if c == 0:
                            i = vec.scalar_tensor_tensor(
                                wacc[:, tt * 512:(tt + 1) * 512], pW[0][:, :], 1.0,
                                f2s[:], ALU.mult, ALU.mult)
                        else:
                            vec.scalar_tensor_tensor(wtmp[:], pW[0][:, :], 1.0,
                                                     f2s[:], ALU.mult, ALU.mult)
                            i = vec.tensor_add(wacc[:, tt * 512:(tt + 1) * 512],
                                               wacc[:, tt * 512:(tt + 1) * 512], wtmp[:])
                    ev_acc = E.inc("v", i if cur == "v" else None)
                    E.res("pW0b")["f"] = ev_acc
            if cur == "v":
                if xi == 0:
                    i = vec.tensor_copy(wrt[:], wacc[:])
                else:
                    vec.tensor_copy(wit[:], wacc[:])
                    i = vec.tensor_scalar(witn[:], wacc[:], -1.0, None, ALU.mult)
            ev = E.inc("v", i if cur == "v" else None)
            if xi == 0:
                ev_wrt = ev
            else:
                ev_wit = ev
        E.res("pW0")["f"] = E.res("pW0b")["f"]
        E.res("pW1")["f"] = E.res("pW1b")["f"]
        if debug:
            if cur == "g":
                gps.wait_ge(SEMH["v"], ev_wit[1])
                gps.dma_start(dbg["dwr"], wrt[:]).then_inc(SEMH["do"], 16)
                gps.dma_start(dbg["dwi"], wit[:]).then_inc(SEMH["do"], 16)
            E.c["do"] += 32

        def emit_qkv(b, q):
            parity = q % 2
            for T in range(4):
                gt = q * 4 + T
                slot = gt % 8
                sidx = gt % 2
                rxs = E.res(f"xs{sidx}")
                if cur == "g":
                    E.wait(rxs["f"])
                i = gps.dma_start(xs[sidx][:],
                                  x_d[b * 2048 + gt * 128: b * 2048 + (gt + 1) * 128, :]) \
                    if cur == "g" else None
                rxs["r"] = E.inc("dx", i, 16)
                rslot = E.res(f"xtm{slot}")
                if cur == "s":
                    E.wait(rxs["r"])
                    E.wait(rslot["f"])
                i = sca.copy(x_tm[:, slot * D:(slot + 1) * D], xs[sidx][:]) \
                    if cur == "s" else None
                ev_conv = E.inc("s", i)
                rxs["f"] = ev_conv
                rslot["r"] = ev_conv
                rpt = E.res("pT")
                if cur == "t":
                    E.wait(ev_conv)
                    E.wait(rpt["f"])
                for c in range(8):
                    i = ten.transpose(pT[:, c * 128:(c + 1) * 128],
                                      x_tm[:, slot * D + c * 128: slot * D + (c + 1) * 128],
                                      identb[:]) if cur == "t" else None
                    if c == 7:
                        ev_tr = E.inc("t", i)
                if cur == "v":
                    E.wait(ev_tr)
                    if b == 0 and q == 0:
                        E.wait(ev_init_v)
                    E.wait(E.res("XTfree")["f"])
                i = vec.tensor_copy(
                    XT[:, T * 1024:(T + 1) * 1024],
                    pT[:, :]) if cur == "v" else None
                ev_evac = E.inc("v", i)
                rpt["f"] = ev_evac
                E.res("XT")["r"] = ev_evac

            def rhsXT(c):
                return ap3(XT, c * 128, [[4096, 128], [1024, 4], [1, 128]])

            ev_xt_ready = E.res("XT")["r"]
            for t in range(8):
                bank = t % 2
                rb = E.res(f"pW{bank}")
                if cur == "t":
                    E.wait(ev_xt_ready)
                    if b == 0 and q == 0 and t < 2:
                        E.wait(wconv_ev["wq"])
                    E.wait(rb["f"])
                for c in range(8):
                    i = ten.matmul(pW[bank][:, :],
                                   wq_sb[:, c * 1024 + t * 128: c * 1024 + (t + 1) * 128],
                                   rhsXT(c), start=(c == 0), stop=(c == 7)) \
                        if cur == "t" else None
                ev_mm = E.inc("t", i if cur == "t" else None)
                rqt = E.res(f"QT{parity}")
                if cur == "s":
                    E.wait(ev_mm)
                    E.wait(rqt["f"])
                    i = sca.add(
                        QT[:, parity * 4096 + t * 512: parity * 4096 + (t + 1) * 512],
                        pW[bank][:, :], bq_sb[:, t:t + 1])
                ev_ev = E.inc("s", i if cur == "s" else None)
                rb["f"] = ev_ev
                if t == 7:
                    rqt["r"] = ev_ev
                    rqt["f"] = None
            rb = E.res("pB")
            if cur == "t":
                E.wait(rb["f"])
                if b == 0 and q == 0:
                    E.wait(wconv_ev["wv"])
            for c in range(8):
                i = ten.matmul(pB[:, :], wk_sb[:, c * 128:(c + 1) * 128], rhsXT(c),
                               start=(c == 0), stop=(c == 7)) if cur == "t" else None
            ev_mm = E.inc("t", i if cur == "t" else None)
            if cur == "s":
                E.wait(ev_mm)
                E.wait(E.res("KTfree")["f"])
                i = sca.add(KT[:, 128 + q * TQ: 128 + (q + 1) * TQ], pB[:, :], bk_sb[:])
            ev_kev = E.inc("s", i if cur == "s" else None)
            rb["f"] = ev_kev
            E.res(f"KT{q}")["r"] = ev_kev
            if cur == "t":
                E.wait(rb["f"])
            for c in range(8):
                i = ten.matmul(pB[:, :], wv_sb[:, c * 128:(c + 1) * 128], rhsXT(c),
                               start=(c == 0), stop=(c == 7)) if cur == "t" else None
            ev_vmm = E.inc("t", i if cur == "t" else None)
            E.res("XTfree")["f"] = ev_vmm
            if cur == "s":
                E.wait(ev_vmm)
                E.wait(E.res("vstage")["f"])
                i = sca.add(vstage[:], pB[:, :], bv_sb[:])
            ev_vev = E.inc("s", i if cur == "s" else None)
            rb["f"] = ev_vev
            rpt = E.res("pT")
            ev_vbev = None
            for T in range(4):
                kb = q * 4 + T + 1
                if cur == "t":
                    E.wait(ev_vev)
                    E.wait(rpt["f"])
                i = ten.transpose(pT[:, 0:128], vstage[:, T * 128:(T + 1) * 128],
                                  identb[:]) if cur == "t" else None
                ev_tr = E.inc("t", i)
                if cur == "v":
                    E.wait(ev_tr)
                    E.wait(E.res("VBfree")["f"])
                i = vec.tensor_copy(ap3(VB, kb * 130, [[2340, 128], [65, 2], [1, 64]]),
                                    ap3(pT, 0, [[1024, 128], [64, 2], [1, 64]])) \
                    if cur == "v" else None
                ev_vbev = E.inc("v", i)
                rpt["f"] = ev_vbev
                if T == 3:
                    E.res("vstage")["f"] = ev_tr
            E.res(f"VB{q}")["r"] = ev_vbev

        quadcol = {0: 0, 1: 512, 2: 256, 3: 768}

        def sc_unit(b, S, c, pair):
            parity = (S // 4) % 2
            kb = S - 1 + c
            qready = E.res(f"QT{parity}")["r"]
            kready = E.res(f"KT{S // 4}")["r"]
            knext = E.res(f"KT{min(S // 4 + 1, 3)}")["r"]
            i = None
            for gq in range(2):
                quad = pair * 2 + gq
                rowbase = 64 * gq
                tilegrp = pair * 2048
                if cur == "t":
                    if gq == 0:
                        if b == 0 and S == 0 and c == 0:
                            E.wait(ev_init_v)
                        if debug and b == 0 and S in (1, 2) and c == 0 and pair == 0:
                            E.wait(E.ev.get("dbg0"))
                        E.wait(qready)
                        E.wait(kready)
                        E.wait(knext)
                    E.wait(E.res(f"scb{quad % 2}")["f"])
                rhs = ap3(QT, rowbase * 8192 + parity * 4096 + tilegrp
                          + (S % 4) * 128, [[8192, 64], [512, 4], [1, 128]])
                lhsT = KT[rowbase:rowbase + 64, (kb + 1) * 128:(kb + 2) * 128]
                i = ten.matmul(pS[quad % 2][:, :], lhsT, rhs, start=True, stop=True,
                               tile_position=(rowbase, 0)) if cur == "t" else None
            ev_sc = E.inc("t", i if cur == "t" else None)
            E.ev[f"sc_{S % 2}_{c}_{pair}"] = ev_sc
            if S % 4 == 3 and c == 2 and pair == 1:
                E.res(f"QT{parity}")["f"] = ev_sc

        def exp_unit(b, S, c, pair):
            ev_sc = E.ev[f"sc_{S % 2}_{c}_{pair}"]
            for gq in range(2):
                quad = pair * 2 + gq
                exi = (S % 2) * 12 + c * 4 + quad
                rex = E.res(f"exfree{exi}")
                if cur == "s":
                    if b == 0 and S == 0 and c == 0 and pair == 0 and gq == 0:
                        sca.wait_ge(SEMH["v"], ev_init_v[1])
                    E.wait(ev_sc)
                    E.wait(rex["f"])
                    i = sca.activation(EX[:, exi * 512:(exi + 1) * 512],
                                       pS[quad % 2][:, :], AF.Exp, bias=expbias[:])
                ev_e = E.inc("s", i if cur == "s" else None)
                E.ev[f"exp_{S % 2}_{c}_{quad}"] = ev_e
                if gq == 1:
                    E.res(f"scb0")["f"] = E.ev[f"exp_{S % 2}_{c}_{pair * 2}"]
                    E.res(f"scb1")["f"] = ev_e

        def mask_unit(b, S, c, pair):
            m = maskl if c == 0 else maskr
            for gq in range(2):
                quad = pair * 2 + gq
                exi = (S % 2) * 12 + c * 4 + quad
                ev_e = E.ev[f"exp_{S % 2}_{c}_{quad}"]
                if cur == "g":
                    E.wait(ev_e)
                i = gps.tensor_mul(EX[:, exi * 512:(exi + 1) * 512],
                                   EX[:, exi * 512:(exi + 1) * 512], m[:]) \
                    if cur == "g" else None
                E.ev[f"msk_{S % 2}_{c}_{quad}"] = E.inc("g", i)

        def av_unit(b, S, quad):
            g = quad % 2
            bank = 2 + quad % 2
            vready = E.res(f"VB{S // 4}")["r"]
            vnext = E.res(f"VB{min(S // 4 + 1, 3)}")["r"]
            if cur == "t":
                E.wait(E.res(f"avb{quad % 2}")["f"])
                E.wait(vready)
                E.wait(vnext)
                for c in range(3):
                    key = ("msk" if c != 1 else "exp") + f"_{S % 2}_{c}_{quad}"
                    E.wait(E.ev[key])
            first = True
            i = None
            for c in range(3):
                kb = S - 1 + c
                exi = (S % 2) * 12 + c * 4 + quad
                for hh in range(4):
                    i = ten.matmul(
                        pS[bank][:, hh * 65:(hh + 1) * 65],
                        EX[:, exi * 512 + hh * 128: exi * 512 + (hh + 1) * 128],
                        VB[:, (kb + 1) * 130 + g * 65: (kb + 1) * 130 + (g + 1) * 65],
                        start=first, stop=(c == 2 and hh == 3)) if cur == "t" else None
                    first = False
            ev = E.inc("t", i if cur == "t" else None)
            E.ev[f"av_{S % 2}_{quad}"] = ev
            for c in range(3):
                E.res(f"exfree{(S % 2) * 12 + c * 4 + quad}")["f"] = ev

        def recip_unit(b, S, quad):
            ring = S % 2
            if cur == "v":
                E.wait(E.ev[f"av_{S % 2}_{quad}"])
                if quad == 0:
                    E.wait(E.res(f"attn{ring}")["f"])
                vec.reciprocal(Rrec[:, ring * 16 + quad * 4: ring * 16 + (quad + 1) * 4],
                               ap3(pS[2 + quad % 2], 64, [[512, 128], [65, 4]]))

        def normts_unit(b, S, quad):
            ring = S % 2
            bank = 2 + quad % 2
            i = None
            for hh in range(4):
                i = vec.tensor_scalar(
                    attn[:, ring * 1024 + quadcol[quad] + hh * 64:
                         ring * 1024 + quadcol[quad] + (hh + 1) * 64],
                    ap3(pS[bank], hh * 65, [[512, 128], [1, 64]]),
                    Rrec[:, ring * 16 + quad * 4 + hh: ring * 16 + quad * 4 + hh + 1],
                    None, ALU.mult) if cur == "v" else None
            ev_nq = E.inc("v", i)
            E.res(f"avb{quad % 2}")["f"] = ev_nq
            if quad == 3:
                E.res(f"attn{ring}")["r"] = ev_nq
                E.ev[f"nq_{ring}"] = ev_nq

        def attnT_unit(b, S):
            ring = S % 2
            rat = E.res(f"attn{ring}")
            rpt = E.res("pT")
            rt = E.res(f"attnT{ring}")
            i = None
            if cur == "t":
                E.wait(rat["r"])
                E.wait(rpt["f"])
            for c in range(8):
                i = ten.transpose(pT[:, c * 128:(c + 1) * 128],
                                  attn[:, ring * 1024 + c * 128: ring * 1024 + (c + 1) * 128],
                                  identb[:]) if cur == "t" else None
            ev_tr = E.inc("t", i if cur == "t" else None)
            rat["f"] = ev_tr
            if cur == "v":
                E.wait(ev_tr)
                E.wait(rt["f"])
            i = vec.tensor_copy(attnT[:, ring * 1024:(ring + 1) * 1024], pT[:, :]) \
                if cur == "v" else None
            ev_atev = E.inc("v", i)
            rpt["f"] = ev_atev
            rt["r"] = ev_atev

        def wo_unit(b, S, half):
            ring = S % 2
            rt = E.res(f"attnT{ring}")
            rb = E.res(f"pW{half}")
            if cur == "t":
                E.wait(rt["r"])
                E.wait(rb["f"])
                if b == 0 and S == 0:
                    E.wait(ev_borow)
                    E.wait(wconv_ev["wo"])
            for c in range(8):
                ten.matmul(pW[half][:, :],
                           attnT[:, ring * 1024 + c * 128: ring * 1024 + (c + 1) * 128],
                           wo_sb[:, c * 1024 + half * 512: c * 1024 + (half + 1) * 512],
                           start=(c == 0), stop=False) if cur == "t" else None
            i = ten.matmul(pW[half][:, :], ones_row[:],
                           bo_row[0:1, half * 512:(half + 1) * 512],
                           start=False, stop=True) if cur == "t" else None
            E.ev[f"wo_{ring}_{half}"] = E.inc("t", i if cur == "t" else None)
            if half == 1:
                rt["f"] = E.ev[f"wo_{ring}_{half}"]

        def lnstats(stt, c0, pre_ap):
            vec.scalar_tensor_tensor(scr[:], pre_ap, 1.0, pre_ap, ALU.mult, ALU.mult,
                                     accum_out=stt[:, c0 + 2:c0 + 3])
            vec.tensor_scalar(stt[:, c0 + 3:c0 + 4], stt[:, c0 + 0:c0 + 1],
                              stt[:, c0 + 1:c0 + 2], -1.0 / 1024.0,
                              ALU.add, ALU.mult)
            vec.tensor_scalar(stt[:, c0 + 5:c0 + 6], stt[:, c0 + 2:c0 + 3],
                              1.0 / 1024.0, EPS, ALU.mult, ALU.add)
            vec.tensor_tensor(stt[:, c0 + 4:c0 + 5], stt[:, c0 + 3:c0 + 4],
                              stt[:, c0 + 3:c0 + 4], ALU.mult)
            vec.tensor_copy(wtmp[:, 0:1], stt[:, c0 + 0:c0 + 1])
            vec.tensor_tensor(stt[:, c0 + 6:c0 + 7], stt[:, c0 + 5:c0 + 6],
                              stt[:, c0 + 4:c0 + 5], ALU.subtract)
            return vec.tensor_copy(wtmp[:, 1:2], stt[:, c0 + 0:c0 + 1])

        def ln1_unit(b, S):
            ring = S % 2
            slot = S % 8
            c0 = ring * 8
            for half in range(2):
                if cur == "v":
                    E.wait(E.ev[f"wo_{ring}_{half}"])
                    E.wait(E.res(f"xtm{slot}")["r"])
                i = vec.scalar_tensor_tensor(
                    pre1[:, ring * 1024 + half * 512: ring * 1024 + (half + 1) * 512],
                    pW[half][:, :], 1.0,
                    x_tm[:, slot * D + half * 512: slot * D + (half + 1) * 512],
                    ALU.mult, ALU.add,
                    accum_out=st1[:, c0 + half: c0 + half + 1]) if cur == "v" else None
                ev_add = E.inc("v", i)
                E.res(f"pW{half}")["f"] = ev_add
            i = None
            if cur == "v":
                i = lnstats(st1, c0, pre1[:, ring * 1024:(ring + 1) * 1024])
            ev_var = E.inc("v", i if cur == "v" else None)
            E.ev[f"var1_{ring}"] = ev_var

        def rstd1_unit(b, S):
            ring = S % 2
            c0 = ring * 8
            if cur == "s":
                E.wait(E.ev[f"var1_{ring}"])
                sca.activation(st1[:, c0 + 7:c0 + 8], st1[:, c0 + 6:c0 + 7], AF.Ln)
                sca.activation(rstd1[:, ring:ring + 1], st1[:, c0 + 7:c0 + 8],
                               AF.Exp, scale=neghalf[:])
                i = sca.copy(st1[:, c0 + 7:c0 + 8], st1[:, c0 + 6:c0 + 7])
            E.ev[f"rstd1_{ring}"] = E.inc("s", i if cur == "s" else None)

        def x1norm_unit(b, S):
            ring = S % 2
            slot = S % 8
            c0 = ring * 8
            rx1 = E.res(f"x1{ring}")
            if cur == "v":
                E.wait(E.ev[f"rstd1_{ring}"])
                E.wait(rx1["f"])
            i = vec.tensor_scalar(x1_tm[:, ring * 1024:(ring + 1) * 1024],
                                  pre1[:, ring * 1024:(ring + 1) * 1024],
                                  st1[:, c0 + 3: c0 + 4],
                                  rstd1[:, ring:ring + 1], ALU.add, ALU.mult) \
                if cur == "v" else None
            ev_x1 = E.inc("v", i)
            E.res(f"xtm{slot}")["f"] = ev_x1
            rx1["r"] = ev_x1
            E.ev[f"x1ev_{ring}"] = ev_x1

        def x1t_unit(b, S):
            ring = S % 2
            rpt = E.res("pT")
            rxt = E.res(f"x1T{ring}")
            rx1 = E.res(f"x1{ring}")
            i = None
            if cur == "t":
                E.wait(rx1["r"])
                E.wait(rpt["f"])
            for c in range(8):
                i = ten.transpose(pT[:, c * 128:(c + 1) * 128],
                                  x1_tm[:, ring * 1024 + c * 128: ring * 1024 + (c + 1) * 128],
                                  identb[:]) if cur == "t" else None
            ev_tr = E.inc("t", i if cur == "t" else None)
            if cur == "v":
                E.wait(ev_tr)
                E.wait(rxt["f"])
            i = vec.tensor_copy(x1T[:, ring * 1024:(ring + 1) * 1024], pT[:, :]) \
                if cur == "v" else None
            ev_x1t = E.inc("v", i)
            rpt["f"] = ev_x1t
            rxt["r"] = ev_x1t

        def ffn_unit(b, S, half):
            ring = S % 2
            rxt = E.res(f"x1T{ring}")
            rb = E.res(f"pW{half}")
            if cur == "t":
                E.wait(rxt["r"])
                E.wait(rb["f"])
                if b == 0 and S == 0:
                    E.wait([ev_wrt, ev_wit, ev_betarow])
            for c in range(8):
                if c < 4:
                    wsrc = wrt if half == 0 else wit
                    wc = c
                else:
                    wsrc = witn if half == 0 else wrt
                    wc = c - 4
                ten.matmul(pW[half][:, :],
                           x1T[:, ring * 1024 + c * 128: ring * 1024 + (c + 1) * 128],
                           wsrc[:, wc * 512:(wc + 1) * 512],
                           start=(c == 0), stop=False) if cur == "t" else None
            i = ten.matmul(pW[half][:, :], ones_row[:],
                           beta_row[0:1, half * 512:(half + 1) * 512],
                           start=False, stop=True) if cur == "t" else None
            E.ev[f"ffn_{ring}_{half}"] = E.inc("t", i if cur == "t" else None)
            if half == 1:
                rxt["f"] = E.ev[f"ffn_{ring}_{half}"]

        def ln2out_unit(b, S):
            ring = S % 2
            c0 = ring * 8
            rx1 = E.res(f"x1{ring}")
            ev_add = None
            for half in range(2):
                if cur == "v":
                    E.wait(E.ev[f"ffn_{ring}_{half}"])
                i = vec.scalar_tensor_tensor(
                    pre2[:, ring * 1024 + half * 512: ring * 1024 + (half + 1) * 512],
                    pW[half][:, :], 1.0,
                    x1_tm[:, ring * 1024 + half * 512: ring * 1024 + (half + 1) * 512],
                    ALU.mult, ALU.add,
                    accum_out=st2[:, c0 + half: c0 + half + 1]) if cur == "v" else None
                ev_add = E.inc("v", i)
                E.res(f"pW{half}")["f"] = ev_add
            rx1["f"] = ev_add
            i = None
            if cur == "v":
                i = lnstats(st2, c0, pre2[:, ring * 1024:(ring + 1) * 1024])
            ev_var = E.inc("v", i if cur == "v" else None)
            if cur == "s":
                E.wait(ev_var)
                sca.activation(st2[:, c0 + 7:c0 + 8], st2[:, c0 + 6:c0 + 7], AF.Ln)
                sca.activation(rstd2[:, ring:ring + 1], st2[:, c0 + 7:c0 + 8],
                               AF.Exp, scale=neghalf[:])
                i = sca.copy(st2[:, c0 + 7:c0 + 8], st2[:, c0 + 6:c0 + 7])
            ev_rstd = E.inc("s", i if cur == "s" else None)
            ro = E.res(f"outst{ring}")
            if cur == "v":
                E.wait(ev_rstd)
                E.wait(ro["f"])
            i = vec.tensor_scalar(outst[:, ring * 1024:(ring + 1) * 1024],
                                  pre2[:, ring * 1024:(ring + 1) * 1024],
                                  st2[:, c0 + 3: c0 + 4],
                                  rstd2[:, ring:ring + 1], ALU.add, ALU.mult) \
                if cur == "v" else None
            ev_out = E.inc("v", i)
            if cur == "g":
                E.wait(ev_out)
            i = gps.dma_start(y_d[b * 2048 + S * 128: b * 2048 + (S + 1) * 128, :],
                              outst[:, ring * 1024:(ring + 1) * 1024]) \
                if cur == "g" else None
            ro["f"] = E.inc("do", i, 16)

            if debug and b == 0 and S == 0:
                if cur == "g":
                    gps.wait_ge(SEMH["v"], E.ev["nq_0"][1])
                    gps.dma_start(dbg["dex"], EX[:, 0:6144]).then_inc(SEMH["do"], 16)
                    gps.dma_start(dbg["dat"], attn[:, 0:1024]).then_inc(SEMH["do"], 16)
                    gps.wait_ge(SEMH["v"], E.ev["x1ev_0"][1])
                    gps.dma_start(dbg["dp1"], pre1[:, 0:1024]).then_inc(SEMH["do"], 16)
                    gps.dma_start(dbg["dx1"], x1_tm[:, 0:1024]).then_inc(SEMH["do"], 16)
                    gps.dma_start(dbg["dst"], st1[:]).then_inc(SEMH["do"], 16)
                    gps.dma_start(dbg["drs"], rstd1[:]).then_inc(SEMH["do"], 16)
                E.c["do"] += 96
                E.ev["dbg0"] = ("do", E.c["do"])

        for b in range(NB):
            if b > 0:
                E.res("KTfree")["f"] = E.ev["lastAV"]["f"]
                E.res("VBfree")["f"] = E.ev["lastAV"]["f"]
            emit_qkv(b, 0)
            emit_qkv(b, 1)
            if debug and b == 0:
                if cur == "g":
                    gps.wait_ge(SEMH["s"], E.res("QT1")["r"][1])
                    gps.dma_start(dbg["dqt"], QT[:]).then_inc(SEMH["do"], 16)
                E.c["do"] += 16
            for k in range(18):
                S, P, Q2 = k, k - 1, k - 2
                if S < 16:
                    sc_unit(b, S, 0, 0)
                    exp_unit(b, S, 0, 0)
                if 0 <= Q2 < 16:
                    x1t_unit(b, Q2)
                if 0 <= P < 16:
                    av_unit(b, P, 0)
                if S < 16:
                    sc_unit(b, S, 0, 1)
                    exp_unit(b, S, 0, 1)
                    mask_unit(b, S, 0, 0)
                    mask_unit(b, S, 0, 1)
                if 0 <= P < 16:
                    av_unit(b, P, 1)
                if 0 <= Q2 < 16:
                    ffn_unit(b, Q2, 0)
                if 0 <= P < 16:
                    recip_unit(b, P, 0)
                    recip_unit(b, P, 1)
                    if cur == "v":
                        vec.tensor_copy(wtmp[:, 2:3], Rrec[:, 0:1])
                    normts_unit(b, P, 0)
                    normts_unit(b, P, 1)
                if S < 16:
                    sc_unit(b, S, 1, 0)
                    exp_unit(b, S, 1, 0)
                if 0 <= P < 16:
                    av_unit(b, P, 2)
                if 0 <= Q2 < 16:
                    ffn_unit(b, Q2, 1)
                if S < 16:
                    sc_unit(b, S, 1, 1)
                    exp_unit(b, S, 1, 1)
                if 0 <= P < 16:
                    av_unit(b, P, 3)
                    recip_unit(b, P, 2)
                    recip_unit(b, P, 3)
                    if cur == "v":
                        vec.tensor_copy(wtmp[:, 2:3], Rrec[:, 0:1])
                    normts_unit(b, P, 2)
                    normts_unit(b, P, 3)
                    attnT_unit(b, P)
                if S < 16:
                    sc_unit(b, S, 2, 0)
                    exp_unit(b, S, 2, 0)
                    mask_unit(b, S, 2, 0)
                if 0 <= Q2 < 16:
                    ln2out_unit(b, Q2)
                if 0 <= P < 16:
                    wo_unit(b, P, 0)
                if S < 16:
                    sc_unit(b, S, 2, 1)
                    exp_unit(b, S, 2, 1)
                    mask_unit(b, S, 2, 1)
                if 0 <= P < 16:
                    wo_unit(b, P, 1)
                    ln1_unit(b, P)
                    rstd1_unit(b, P)
                    x1norm_unit(b, P)
                if k == 5:
                    emit_qkv(b, 2)
                if k == 9:
                    emit_qkv(b, 3)
                if debug and b == 0 and k == 9:
                    if cur == "g":
                        gps.wait_ge(SEMH["s"], E.res("KT3")["r"][1])
                        gps.wait_ge(SEMH["v"], E.res("VB3")["r"][1])
                        gps.dma_start(dbg["dkt"], KT[:]).then_inc(SEMH["do"], 16)
                        gps.dma_start(dbg["dvb"], VB[:]).then_inc(SEMH["do"], 16)
                    E.c["do"] += 32
            E.ev["lastAV"] = {"f": ("t", E.c["t"])}

        if cur == "g":
            gps.wait_ge(SEMH["do"], E.c["do"])

    with nc.Block() as blk:
        @blk.sync
        def _(sync):
            schedule(Em("sync", sync))

        @blk.tensor
        def _(tensor):
            schedule(Em("t", tensor))

        @blk.scalar
        def _(scalar):
            schedule(Em("s", scalar))

        @blk.vector
        def _(vector):
            schedule(Em("v", vector))

        @blk.gpsimd
        def _(gpsimd):
            schedule(Em("g", gpsimd))

    return nc


def _consts():
    f = np.arange(512)
    p = np.arange(128)[:, None]
    base_l = (p >= (f[None, :] % 128)).astype(np.float32)
    base_r = (p < (f[None, :] % 128)).astype(np.float32)
    r32 = (f[None, :] // 16 == np.arange(32)[:, None]).astype(np.float32)
    d32 = np.zeros((32, 512), np.float32)
    for t in range(4):
        for pl in range(128):
            d32[t * 8 + pl // 16, t * 128 + pl] = 1.0
    r16 = (f[None, :] % 16 == np.arange(16)[:, None]).astype(np.float32)
    d16 = (np.arange(128)[None, :] % 16 == np.arange(16)[:, None]).astype(np.float32)
    bf = ml_dtypes.bfloat16
    return dict(ident=np.eye(128, dtype=np.float32).astype(bf),
                maskl=base_l.astype(bf), maskr=base_r.astype(bf),
                r32=r32.astype(bf), d32=d32.astype(bf),
                r16=r16.astype(bf), d16=d16.astype(bf))


def _in_maps(L0_attn_x, L0_attn_Wq, L0_attn_bq, L0_attn_Wk, L0_attn_bk,
             L0_attn_Wv, L0_attn_bv, L0_attn_Wo, L0_attn_bo,
             L0_ffn_G1_R, L0_ffn_G2_R, L0_ffn_G1_I, L0_ffn_G2_I,
             L0_ffn_bias_real, L0_ffn_bias_imag):
    x = np.ascontiguousarray(np.asarray(L0_attn_x, dtype=np.float32))
    perm = np.concatenate([np.concatenate([np.arange(64 * t, 64 * t + 64),
                                           np.arange(512 + 64 * t, 512 + 64 * t + 64)])
                           for t in range(8)])
    wq = np.ascontiguousarray(np.asarray(L0_attn_Wq, np.float32)[:, perm])
    bq = np.ascontiguousarray(np.asarray(L0_attn_bq, np.float32)[perm])
    cst = _consts()
    in_maps = []
    for c in range(8):
        m = dict(x=np.ascontiguousarray(x[2 * c:2 * c + 2].reshape(4096, 1024)),
                 wq=wq, bq=bq,
                 wk=np.ascontiguousarray(np.asarray(L0_attn_Wk, np.float32)),
                 bk=np.ascontiguousarray(np.asarray(L0_attn_bk, np.float32)),
                 wv=np.ascontiguousarray(np.asarray(L0_attn_Wv, np.float32)),
                 bv=np.ascontiguousarray(np.asarray(L0_attn_bv, np.float32)),
                 wo=np.ascontiguousarray(np.asarray(L0_attn_Wo, np.float32)),
                 bo=np.ascontiguousarray(np.asarray(L0_attn_bo, np.float32)),
                 g1r=np.ascontiguousarray(np.asarray(L0_ffn_G1_R, np.float32).reshape(32, 128)),
                 g2r=np.ascontiguousarray(np.asarray(L0_ffn_G2_R, np.float32).reshape(4, 16, 16)),
                 g1i=np.ascontiguousarray(np.asarray(L0_ffn_G1_I, np.float32).reshape(32, 128)),
                 g2i=np.ascontiguousarray(np.asarray(L0_ffn_G2_I, np.float32).reshape(4, 16, 16)),
                 br=np.ascontiguousarray(np.asarray(L0_ffn_bias_real, np.float32)),
                 bi=np.ascontiguousarray(np.asarray(L0_ffn_bias_imag, np.float32)),
                 **cst)
        in_maps.append(m)
    return in_maps


def kernel(**inputs):
    nc = build_nc(debug=False)
    in_maps = _in_maps(**inputs)
    res = run_bass_kernel_spmd(nc, in_maps, core_ids=list(range(8)))
    out = np.concatenate([r["y"].reshape(2, 2048, 1024) for r in res.results], axis=0)
    return out
